# version 02 (final)
# baseline (speedup 1.0000x reference)
import numpy as np
import concourse.bass as bass
import concourse.mybir as mybir
from concourse.bass_utils import run_bass_kernel_spmd

F32 = mybir.dt.float32
AF = mybir.ActivationFunctionType
ALU = mybir.AluOpType
AX = mybir.AxisListType
DS = bass.ds

SEQ = 8192
D = 1024
NT = 4096
TILE = 512
NTILE = NT // TILE
TC = TILE // 128
DC = D // 128
E = 8
NHG = 16
NH = 8
HD = 64
CH = 128
NCH = SEQ // CH
LOG2 = float(np.log(2.0))
LN_EPS = 1e-5
KN_EPS = 1e-12
RMS_EPS = 1e-6


class KB:

    def __init__(self, debug=False, stages="AMOC"):
        self.stages = stages
        self.debug = debug
        self.nc = bass.Bass("TRN2", target_bir_lowering=False, debug=False,
                            detect_race_conditions=False)
        self.cnt = {}
        self.sems = {}
        self.ctx = []

    def _enter(self, cm):
        h = cm.__enter__()
        self.ctx.append(cm)
        return h

    def sem(self, name):
        self.sems[name] = self._enter(self.nc.semaphore(name))
        self.cnt[name] = 0

    def inc(self, instr, name, n=1):
        instr.then_inc(self.sems[name], n)
        self.cnt[name] += n
        return instr

    def w(self, eng, name, at=None):
        v = self.cnt[name] if at is None else at
        if v > 0:
            eng.wait_ge(self.sems[name], v)

    def sbuf(self, name, shape, dtype=F32):
        return self._enter(self.nc.sbuf_tensor(name, shape, dtype))

    def psumt(self, name, shape):
        return self._enter(self.nc.psum_tensor(name, shape, F32))

    def SY(self, fn):
        self.block.sync(fn)

    def SC(self, fn):
        self.block.scalar(fn)

    def TE(self, fn):
        self.block.tensor(fn)

    def VE(self, fn):
        self.block.vector(fn)

    def GP(self, fn):
        self.block.gpsimd(fn)

    def declare(self):
        nc = self.nc

        def dram(n, s, kind=None):
            if kind:
                return nc.dram_tensor(n, s, F32, kind=kind).ap()
            return nc.dram_tensor(n, s, F32).ap()

        self.xin = dram("xin", [NT, D], "ExternalInput")
        self.ew = dram("ew", [E, D, D], "ExternalInput")
        self.ebd = dram("ebd", [E, D], "ExternalInput")
        self.rwt = dram("rwt", [D, E], "ExternalInput")
        self.rb = dram("rb", [1, E], "ExternalInput")
        self.wq = dram("wq", [D, D], "ExternalInput")
        self.wk = dram("wk", [D, D], "ExternalInput")
        self.wv = dram("wv", [D, D], "ExternalInput")
        self.wo = dram("wo", [D, D], "ExternalInput")
        self.wbet = dram("wbet", [D, NHG], "ExternalInput")
        self.wf = dram("wf", [D, NHG], "ExternalInput")
        self.nw = dram("nw", [1, D], "ExternalInput")
        self.c_ident = dram("c_ident", [128, 128], "ExternalInput")
        self.c_tri = dram("c_tri", [128, 128], "ExternalInput")
        self.c_negstrict = dram("c_negstrict", [128, 128], "ExternalInput")
        self.c_incl = dram("c_incl", [128, 128], "ExternalInput")
        self.c_ones1 = dram("c_ones1", [1, 128], "ExternalInput")
        self.c_indla = dram("c_indla", [NHG, NH * 128], "ExternalInput")
        self.c_negind = dram("c_negind", [NHG, NH * 128], "ExternalInput")
        self.x1h = dram("x1h", [NT, D])
        self.Mq = dram("Mq", [2, NH * HD, NT])
        self.Mk = dram("Mk", [2, NH * HD, NT])
        self.Mv = dram("Mv", [2, NT, NH * HD])
        self.Mlg = dram("Mlg", [2, NT, 16])
        self.Msq = dram("Msq", [NH * HD, NT])
        self.Msk = dram("Msk", [NH * HD, NT])
        self.Msv = dram("Msv", [NT, NH * HD])
        self.Mslg = dram("Mslg", [NT, 16])
        self.agq_i = dram("agq_i", [NTILE, NH * HD, TILE])
        self.agq_o = dram("agq_o", [NTILE, 2, NH * HD, TILE])
        self.agk_i = dram("agk_i", [NTILE, NH * HD, TILE])
        self.agk_o = dram("agk_o", [NTILE, 2, NH * HD, TILE])
        self.agv_i = dram("agv_i", [NTILE, TILE, NH * HD])
        self.agv_o = dram("agv_o", [NTILE, 2, TILE, NH * HD])
        self.aglg_i = dram("aglg_i", [NTILE, TILE, 16])
        self.aglg_o = dram("aglg_o", [NTILE, 2, TILE, 16])
        self.o_all = dram("o_all", [2, NT, NH * HD])
        self.ago_i = dram("ago_i", [4, NT // 4, NH * HD])
        self.ago_o = dram("ago_o", [4, 2, NT // 4, NH * HD])
        self.Mo = dram("Mo", [NT, D])
        self.yout = dram("yout", [NT, D], "ExternalOutput")
        self.dbg = {}
        if self.debug:
            for n, s in [("dbg_x1", [NT, D]), ("dbg_mq", [2, NH * HD, NT]),
                         ("dbg_mk", [2, NH * HD, NT]),
                         ("dbg_mv", [2, NT, NH * HD]),
                         ("dbg_mlg", [2, NT, 16]),
                         ("dbg_o", [2, NT, NH * HD]),
                         ("dbg_mo", [NT, D])]:
                self.dbg[n] = dram(n, s, "ExternalOutput")

        sb = self.sbuf
        self.xres = sb("xres", [128, TC, D])
        self.lntm = sb("lntm", [128, TC, D])
        self.lnfm = sb("lnfm", [128, DC, TILE])
        self.wbuf = sb("wbuf", [128, 2, DC, TILE])
        self.acc = sb("acc", [128, TC, TILE])
        self.qfm = sb("qfm", [128, DC, TILE])
        self.ktm = sb("ktm", [128, TC, D])
        self.kfm = sb("kfm", [128, DC, TILE])
        self.vtm = self.lntm
        self.lgt = sb("lgt", [128, TC, 32])
        self.stat = sb("stat", [128, TC, 26])
        self.mcol = sb("mcol", [128, TC, E])
        self.mraw = sb("mraw", [128, TC, E])
        self.w05 = sb("w05", [128, TC, 1])
        self.sq = sb("sq", [128, D])
        self.zer8 = sb("zer8", [128, E])
        self.epsc = sb("epsc", [128, 3])
        self.kq = sb("kq", [64, 2, NH, CH])
        self.vch = sb("vch", [128, NH * HD])
        self.lgch = sb("lgch", [128, 16])
        self.lalbtm = sb("lalbtm", [128, 16])
        self.lalbfm = sb("lalbfm", [16, 128])
        self.gam = sb("gam", [128, NH])
        self.ngam = sb("ngam", [128, NH])
        self.gend = sb("gend", [128, NH])
        self.laend = sb("laend", [1, NH])
        self.e2 = sb("e2", [128, NH, 128])
        self.gt = sb("gt", [128, NH, 128])
        self.g2t = sb("g2t", [128, NH, 128])
        self.ht = sb("ht", [128, NH, 128])
        self.mt = sb("mt", [128, NH, 128])
        self.bt = sb("bt", [128, NH, HD])
        self.usc = sb("usc", [128, NH, HD])
        self.ktmh = sb("ktmh", [128, NH, HD])
        self.tq = sb("tq", [128, NH, HD])
        self.svst = sb("svst", [64, NH, HD])
        self.osb = sb("osb", [128, NH * HD])
        self.ident = sb("ident", [128, 128])
        self.tri = sb("tri", [128, 128])
        self.negstrict = sb("negstrict", [128, 128])
        self.incl = sb("incl", [128, 128])
        self.ones1 = sb("ones1", [1, 128])
        self.indla = sb("indla", [NHG, NH * 128])
        self.negind = sb("negind", [NHG, NH * 128])
        self.rwts = sb("rwts", [128, DC, E])
        self.rbs = sb("rbs", [1, E])
        self.wbias = sb("wbias", [1, 2, 512])
        self.wbets = sb("wbets", [128, DC, NHG])
        self.wfs = sb("wfs", [128, DC, NHG])
        self.nwrep = sb("nwrep", [128, D])

        self.pA = self.psumt("pA", [128, 512])
        self.pB = self.psumt("pB", [128, 512])
        self.pT = self.psumt("pT", [128, 4, 128])
        self.pS = self.psumt("pS", [128, 512])
        self.pM1 = self.psumt("pM1", [128, 4, 128])
        self.pM2 = self.psumt("pM2", [128, 4, 128])
        self.pM3 = self.psumt("pM3", [128, 8, 64])

        for s in (["qC", "qX", "qX1", "qW0", "qW1", "qS", "qG", "qMO",
                   "CC", "CO", "PE", "DV", "AC", "GPc",
                   "OD0", "OD1", "OD2", "OD3", "ST"]
                  + ["qP%d" % j for j in range(NTILE)]
                  + ["qA%d" % j for j in range(4)]):
            self.sem(s)

    def build(self):
        nc = self.nc
        self.declare()
        self.pid = nc.partition_id()
        self.par = self.pid % 2
        self.dmy = self.par
        self.dot = 1 - self.par
        self.block = self._enter(nc.Block())

        self.asm_done = {}
        self.mo_done = {}

        self.emit_init()
        if "A" in self.stages:
            for j in range(NTILE):
                self.emit_tile_A(j)
            self.emit_m_finalize()
        if "M" in "".join(self.stages.split("|")):
            nmix = NCH
            for s in self.stages.split("|"):
                if s.startswith("M") and len(s) > 1:
                    nmix = int(s[1:])
            for c in range(min(nmix, NCH)):
                self.emit_mixer_chunk(c)
        if "O" in self.stages:
            for jj in range(4):
                self.emit_o_exchange(jj)
            self.emit_mo_assembly()
        if "C" in self.stages:
            for jj in range(NTILE):
                self.emit_tile_C(jj)
        self.emit_final()

        for cm in reversed(self.ctx):
            cm.__exit__(None, None, None)
        return nc

    def emit_init(self):
        def f(sync):
            for dst, src in [
                (self.ident, self.c_ident), (self.tri, self.c_tri),
                (self.negstrict, self.c_negstrict), (self.incl, self.c_incl),
                (self.ones1, self.c_ones1), (self.indla, self.c_indla),
                (self.negind, self.c_negind), (self.rbs, self.rb),
            ]:
                self.inc(sync.dma_start(out=dst[...], in_=src[...]), "qC", 16)
            self.inc(sync.dma_start(
                out=self.rwts[...],
                in_=self.rwt.rearrange("(c p) e -> p c e", p=128)), "qC", 16)
            self.inc(sync.dma_start(
                out=self.wbets[...],
                in_=self.wbet.rearrange("(c p) h -> p c h", p=128)), "qC", 16)
            self.inc(sync.dma_start(
                out=self.wfs[...],
                in_=self.wf.rearrange("(c p) h -> p c h", p=128)), "qC", 16)
            self.inc(sync.dma_start(out=self.sq[0:1, :], in_=self.nw[...]),
                     "qC", 16)
        self.SY(f)
        qL1 = self.cnt["qC"]

        def f(vector):
            vector.memset(self.zer8[...], 0.0)
            vector.memset(self.epsc[:, 0:1], LN_EPS)
            vector.memset(self.epsc[:, 1:2], KN_EPS)
            vector.memset(self.epsc[:, 2:3], RMS_EPS)
            self.inc(vector.memset(self.svst[...], 0.0), "DV")
        self.VE(f)

        for half in range(2):
            def f(tensor, half=half, qL1=qL1):
                if half == 0:
                    self.w(tensor, "qC", qL1)
                else:
                    self.w(tensor, "DV")
                self.inc(tensor.matmul(
                    self.pS[:, 0:512], self.ones1[...],
                    self.sq[0:1, half * 512:half * 512 + 512],
                    start=True, stop=True), "PE")
            self.TE(f)
            peN = self.cnt["PE"]

            def f(vector, half=half, peN=peN):
                self.w(vector, "PE", peN)
                self.inc(vector.tensor_copy(
                    self.nwrep[:, half * 512:half * 512 + 512],
                    self.pS[:, 0:512]), "DV")
            self.VE(f)

    def emit_ln(self, qgate=None, dvgate=False):
        def f(scalar, qgate=qgate, dvgate=dvgate):
            if qgate:
                self.w(scalar, qgate[0], qgate[1])
                self.w(scalar, "PE")
            if qgate or dvgate:
                self.w(scalar, "DV")
            for i in range(TC):
                self.inc(scalar.activation(
                    self.sq[:, 0:D], self.xres[:, i, :], AF.Square,
                    accum_out=self.stat[:, i, 1:2]), "AC")
        self.SC(f)
        ac0 = self.cnt["AC"]

        def f(vector, qgate=qgate):
            if qgate:
                self.w(vector, qgate[0], qgate[1])
            for i in range(TC):
                self.inc(vector.tensor_reduce(
                    self.stat[:, i, 0:1], self.xres[:, i, :], AX.X, ALU.add),
                    "DV")
            self.w(vector, "AC", ac0)
            self.w(vector, "DV")
            for i in range(TC):
                self.inc(vector.tensor_scalar(
                    self.stat[:, i, 2:3], self.stat[:, i, 0:1], 1.0 / D,
                    None, ALU.mult), "DV")
            self.w(vector, "DV")
            for i in range(TC):
                self.inc(vector.tensor_tensor(
                    self.stat[:, i, 3:4], self.stat[:, i, 2:3],
                    self.stat[:, i, 2:3], ALU.mult), "DV")
            self.w(vector, "DV")
            for i in range(TC):
                self.inc(vector.tensor_scalar(
                    self.stat[:, i, 4:5], self.stat[:, i, 1:2], 1.0 / D,
                    self.stat[:, i, 3:4], ALU.mult, ALU.subtract), "DV")
        self.VE(f)
        dv1 = self.cnt["DV"]

        def f(scalar, dv1=dv1):
            self.w(scalar, "DV", dv1)
            for i in range(TC):
                self.inc(scalar.activation(
                    self.stat[:, i, 9:10], self.stat[:, i, 4:5], AF.Sqrt,
                    bias=self.epsc[:, 0:1]), "AC")
        self.SC(f)
        ac1 = self.cnt["AC"]

        def f(vector, ac1=ac1):
            self.w(vector, "AC", ac1)
            for i in range(TC):
                self.inc(vector.reciprocal(self.stat[:, i, 5:6],
                                           self.stat[:, i, 9:10]), "DV")
            self.w(vector, "DV")
            for i in range(TC):
                self.inc(vector.tensor_scalar(
                    self.lntm[:, i, :], self.xres[:, i, :],
                    self.stat[:, i, 2:3], self.stat[:, i, 5:6],
                    ALU.subtract, ALU.mult), "DV")
        self.VE(f)
        dv2 = self.cnt["DV"]
        self.emit_transpose_tile(self.lntm, self.lnfm, ("DV", dv2))

    def emit_transpose_tile(self, src_tm, dst_fm, gate):
        for i in range(TC):
            for dg in range(2):
                def f(tensor, i=i, dg=dg, gate=gate):
                    if i == 0 and dg == 0:
                        self.w(tensor, gate[0], gate[1])
                    else:
                        self.w(tensor, "AC")
                    for d in range(dg * 4, dg * 4 + 4):
                        ins = tensor.transpose(
                            self.pT[:, d % 4, :],
                            src_tm[:, i, d * 128:(d + 1) * 128],
                            self.ident[...])
                    self.inc(ins, "PE")
                self.TE(f)
                peX = self.cnt["PE"]

                def f(scalar, i=i, dg=dg, peX=peX):
                    self.w(scalar, "PE", peX)
                    for d in range(dg * 4, dg * 4 + 4):
                        self.inc(scalar.activation(
                            dst_fm[:, d, i * 128:(i + 1) * 128],
                            self.pT[:, d % 4, :], AF.Copy), "AC")
                self.SC(f)

    def emit_router(self):
        ac_at = self.cnt["AC"]

        def f(tensor):
            self.w(tensor, "AC", ac_at)
            self.w(tensor, "DV")
            for i in range(TC):
                for d in range(DC):
                    tensor.matmul(self.pS[:, i * 8:i * 8 + 8],
                                  self.lnfm[:, d, i * 128:(i + 1) * 128],
                                  self.rwts[:, d, :],
                                  start=(d == 0), stop=False)
                self.inc(tensor.matmul(self.pS[:, i * 8:i * 8 + 8],
                                       self.ones1[:, 0:128], self.rbs[...],
                                       start=False, stop=True), "PE")
        self.TE(f)
        peR = self.cnt["PE"]

        def f(vector):
            self.w(vector, "PE", peR)
            for i in range(TC):
                self.inc(vector.reduce_max(
                    self.stat[:, i, 6:7], self.pS[:, i * 8:i * 8 + 8],
                    AX.X, negate=True), "DV")
        self.VE(f)
        dvR = self.cnt["DV"]

        def f(scalar):
            self.w(scalar, "DV", dvR)
            for i in range(TC):
                self.inc(scalar.activation(
                    self.sq[:, i * 8:i * 8 + 8], self.pS[:, i * 8:i * 8 + 8],
                    AF.Exp, bias=self.stat[:, i, 6:7],
                    accum_out=self.stat[:, i, 7:8]), "AC")
        self.SC(f)
        acR = self.cnt["AC"]

        def f(vector):
            self.w(vector, "AC", acR)
            for i in range(TC):
                self.inc(vector.reciprocal(self.stat[:, i, 8:9],
                                           self.stat[:, i, 7:8]), "DV")
                self.inc(vector.tensor_scalar(
                    self.mraw[:, i, :], self.pS[:, i * 8:i * 8 + 8],
                    self.stat[:, i, 6:7], 0.0, ALU.add, ALU.is_ge), "DV")
            self.w(vector, "DV")
            for i in range(TC):
                self.inc(vector.tensor_scalar(
                    self.w05[:, i, :], self.stat[:, i, 8:9], 0.5, None,
                    ALU.mult), "DV")
                self.inc(vector.tensor_tensor_scan(
                    self.mcol[:, i, :], self.mraw[:, i, :], self.zer8[...],
                    0.0, ALU.add, ALU.add), "DV")
            self.w(vector, "DV")
            for i in range(TC):
                self.inc(vector.tensor_scalar(
                    self.mcol[:, i, :], self.mcol[:, i, :], 1.0, None,
                    ALU.is_equal), "DV")
            self.w(vector, "DV")
            for i in range(TC):
                self.inc(vector.tensor_tensor(self.mcol[:, i, :],
                                              self.mcol[:, i, :],
                                              self.mraw[:, i, :], ALU.mult),
                         "DV")
        self.VE(f)

    def emit_ffn(self, wsrc):
        dv_masks = self.cnt["DV"]
        ac_lnfm = self.cnt["AC"]
        pe_entry = self.cnt["PE"]
        pe_wdone = {}
        dv_sel = {}

        for half in range(2):
            for e in range(E):
                idx = half * E + e
                buf = idx % 2

                wsem = "qW%d" % buf

                def f(sync, e=e, half=half, buf=buf, idx=idx, wsem=wsem):
                    if idx >= 2:
                        self.w(sync, "PE", pe_wdone[idx - 2])
                    else:
                        self.w(sync, "PE", pe_entry)
                    self.inc(sync.dma_start(
                        out=self.wbuf[:, buf, :, :],
                        in_=wsrc(e, half)), wsem, 16)
                    self.inc(sync.dma_start(
                        out=self.wbias[0:1, buf, :],
                        in_=self.ebd[e:e + 1,
                                     half * 512:half * 512 + 512]), wsem, 16)
                self.SY(f)
                qw = self.cnt[wsem]

                for i in range(TC):
                    gid = idx * TC + i
                    ps = [self.pA, self.pB][gid % 2]

                    def f(tensor, e=e, half=half, buf=buf, i=i, gid=gid,
                          ps=ps, qw=qw, idx=idx, wsem=wsem):
                        if i == 0:
                            self.w(tensor, wsem, qw)
                            if idx == 0:
                                self.w(tensor, "AC", ac_lnfm)
                        if gid >= 2:
                            self.w(tensor, "DV", dv_sel[gid - 2])
                        for d in range(DC):
                            tensor.matmul(
                                ps[:, 0:512],
                                self.lnfm[:, d, i * 128:(i + 1) * 128],
                                self.wbuf[:, buf, d, :],
                                start=(d == 0), stop=False)
                        self.inc(tensor.matmul(
                            ps[:, 0:512], self.ones1[:, 0:128],
                            self.wbias[0:1, buf, :],
                            start=False, stop=True), "PE")
                    self.TE(f)
                    peG = self.cnt["PE"]

                    def f(vector, e=e, i=i, gid=gid, ps=ps, peG=peG):
                        self.w(vector, "PE", peG)
                        if e == 0:
                            self.w(vector, "DV", dv_masks)
                            self.inc(vector.tensor_scalar(
                                self.acc[:, i, :], ps[:, 0:512],
                                self.mcol[:, i, e:e + 1], None, ALU.mult),
                                "DV")
                        else:
                            self.inc(vector.scalar_tensor_tensor(
                                self.acc[:, i, :], ps[:, 0:512],
                                self.mcol[:, i, e:e + 1], self.acc[:, i, :],
                                ALU.mult, ALU.add), "DV")
                    self.VE(f)
                    dv_sel[gid] = self.cnt["DV"]
                pe_wdone[idx] = self.cnt["PE"]

            def f(vector, half=half):
                for i in range(TC):
                    self.inc(vector.scalar_tensor_tensor(
                        self.xres[:, i, half * 512:half * 512 + 512],
                        self.acc[:, i, :], self.w05[:, i, :],
                        self.xres[:, i, half * 512:half * 512 + 512],
                        ALU.mult, ALU.add), "DV")
            self.VE(f)

    def emit_tile_A(self, j):
        dv_prev = self.cnt["DV"]
        ac_prev = self.cnt["AC"]

        def f(sync, j=j):
            if j > 0:
                self.w(sync, "DV", dv_prev)
                self.w(sync, "AC", ac_prev)
            self.inc(sync.dma_start(
                out=self.xres[...],
                in_=self.xin[j * TILE:(j + 1) * TILE, :].rearrange(
                    "(i p) d -> p i d", p=128)), "qX", 16)
        self.SY(f)
        qx = self.cnt["qX"]

        if j > 0:
            qs_prev = self.cnt["qS"]

            def f(scalar, qs_prev=qs_prev, j=j):
                self.w(scalar, "qS", qs_prev)
                self.w(scalar, "qP%d" % (j - 1), 64)
            self.SC(f)

            def f(vector, qs_prev=qs_prev, j=j):
                self.w(vector, "qS", qs_prev)
                self.w(vector, "qP%d" % (j - 1), 64)
            self.VE(f)

        self.emit_ln(qgate=("qX", qx))
        self.emit_router()
        self.emit_ffn(lambda e, half: self.ew[e].rearrange(
            "(c p) o -> p c o", p=128)[:, :, half * 512:half * 512 + 512])

        dv_x1 = self.cnt["DV"]

        def f(sync, j=j, dv_x1=dv_x1):
            self.w(sync, "DV", dv_x1)
            self.inc(sync.dma_start(
                out=self.x1h[j * TILE:(j + 1) * TILE, :].rearrange(
                    "(i p) d -> p i d", p=128),
                in_=self.xres[...]), "qX1", 16)
        self.SY(f)

        self.emit_ln(dvgate=True)
        self.emit_proj()
        self.emit_pack_and_ag(j)

    def emit_proj(self):
        ac_lnfm = self.cnt["AC"]
        pe_entry = self.cnt["PE"]
        pe_q = {}
        ac_q = {}

        for half in range(2):
            buf = half

            wsem = "qW%d" % buf

            def f(sync, half=half, buf=buf, wsem=wsem):
                if half == 0:
                    self.w(sync, "PE", pe_entry)
                else:
                    self.w(sync, "PE", pe_q[1])
                self.inc(sync.dma_start(
                    out=self.wbuf[:, buf, :, :],
                    in_=self.wq.rearrange("(c p) o -> p c o", p=128)[
                        :, :, half * 512:half * 512 + 512]), wsem, 16)
            self.SY(f)
            qw = self.cnt[wsem]

            for oo in range(4):
                od = half * 4 + oo

                def f(tensor, od=od, oo=oo, buf=buf, qw=qw, half=half,
                      wsem=wsem):
                    self.w(tensor, wsem, qw)
                    if od == 0:
                        self.w(tensor, "AC", ac_lnfm)
                        self.w(tensor, "DV")
                    if od >= 2:
                        self.w(tensor, "AC", ac_q[od - 2])
                    ps = [self.pA, self.pB][od % 2]
                    for d in range(DC):
                        ins = tensor.matmul(
                            ps[:, 0:512],
                            self.wbuf[:, buf, d, oo * 128:(oo + 1) * 128],
                            self.lnfm[:, d, :],
                            start=(d == 0), stop=(d == DC - 1))
                    self.inc(ins, "PE")
                self.TE(f)
                pe_q[od] = self.cnt["PE"]

                def f(scalar, od=od):
                    self.w(scalar, "PE", pe_q[od])
                    ps = [self.pA, self.pB][od % 2]
                    self.inc(scalar.activation(self.qfm[:, od, :],
                                               ps[:, 0:512], AF.Silu), "AC")
                self.SC(f)
                ac_q[od] = self.cnt["AC"]

        pe_kv = {}
        ac_kv = {}
        for which, wsrc, dst, act in [
            ("k", self.wk, self.ktm, AF.Silu),
            ("v", self.wv, self.vtm, AF.Copy),
        ]:
            for half in range(2):
                buf = half

                wsem = "qW%d" % buf

                def f(sync, wsrc=wsrc, half=half, buf=buf, wsem=wsem):
                    self.w(sync, "PE")
                    self.inc(sync.dma_start(
                        out=self.wbuf[:, buf, :, :],
                        in_=wsrc.rearrange("(c p) o -> p c o", p=128)[
                            :, :, half * 512:half * 512 + 512]), wsem, 16)
                self.SY(f)
                qw = self.cnt[wsem]

                for i in range(TC):
                    ps = [self.pA, self.pB][i % 2]

                    def f(tensor, i=i, half=half, buf=buf, qw=qw, ps=ps,
                          which=which, wsem=wsem):
                        if i == 0:
                            self.w(tensor, wsem, qw)
                            self.w(tensor, "AC")
                        if i >= 2:
                            self.w(tensor, "AC", ac_kv[(which, half, i - 2)])
                        for d in range(DC):
                            ins = tensor.matmul(
                                ps[:, 0:512],
                                self.lnfm[:, d, i * 128:(i + 1) * 128],
                                self.wbuf[:, buf, d, :],
                                start=(d == 0), stop=(d == DC - 1))
                        self.inc(ins, "PE")
                    self.TE(f)
                    pe_kv[(which, half, i)] = self.cnt["PE"]

                    def f(scalar, i=i, half=half, ps=ps, which=which,
                          dst=dst, act=act):
                        self.w(scalar, "PE", pe_kv[(which, half, i)])
                        self.inc(scalar.activation(
                            dst[:, i, half * 512:half * 512 + 512],
                            ps[:, 0:512], act), "AC")
                    self.SC(f)
                    ac_kv[(which, half, i)] = self.cnt["AC"]

        def f(tensor):
            for i in range(TC):
                for d in range(DC):
                    tensor.matmul(self.pS[:, i * 32:i * 32 + 16],
                                  self.lnfm[:, d, i * 128:(i + 1) * 128],
                                  self.wbets[:, d, :],
                                  start=(d == 0), stop=(d == DC - 1))
                for d in range(DC):
                    ins = tensor.matmul(self.pS[:, i * 32 + 16:i * 32 + 32],
                                        self.lnfm[:, d, i * 128:(i + 1) * 128],
                                        self.wfs[:, d, :],
                                        start=(d == 0), stop=(d == DC - 1))
                self.inc(ins, "PE")
        self.TE(f)
        peZ = self.cnt["PE"]

        def f(scalar):
            self.w(scalar, "PE", peZ)
            for i in range(TC):
                self.inc(scalar.activation(
                    self.sq[:, i * 32:i * 32 + 16],
                    self.pS[:, i * 32 + 16:i * 32 + 32], AF.Sigmoid), "AC")
                self.inc(scalar.activation(
                    self.sq[:, i * 32 + 16:i * 32 + 32],
                    self.pS[:, i * 32:i * 32 + 16], AF.Sigmoid), "AC")
            self.w(scalar, "AC")
            for i in range(TC):
                self.inc(scalar.activation(
                    self.lgt[:, i, 0:8], self.sq[:, i * 32:i * 32 + 8],
                    AF.Ln), "AC")
                self.inc(scalar.activation(
                    self.lgt[:, i, 16:24],
                    self.sq[:, i * 32 + 8:i * 32 + 16], AF.Ln), "AC")
                self.inc(scalar.activation(
                    self.sq[:, 128 + i * 16:128 + i * 16 + 16],
                    self.sq[:, i * 32 + 16:i * 32 + 32], AF.Ln), "AC")
        self.SC(f)
        acZ = self.cnt["AC"]

        def f(vector):
            self.w(vector, "AC", acZ)
            for i in range(TC):
                self.inc(vector.tensor_scalar(
                    self.lgt[:, i, 8:16],
                    self.sq[:, 128 + i * 16:128 + i * 16 + 8],
                    1.0, LOG2, ALU.mult, ALU.add), "DV")
                self.inc(vector.tensor_scalar(
                    self.lgt[:, i, 24:32],
                    self.sq[:, 128 + i * 16 + 8:128 + i * 16 + 16],
                    1.0, LOG2, ALU.mult, ALU.add), "DV")
        self.VE(f)

        for i in range(TC):
            def f(scalar, i=i):
                self.w(scalar, "DV")
                self.inc(scalar.activation(
                    self.sq[:, 0:D], self.ktm[:, i, :], AF.Square), "AC")
            self.SC(f)
            acS = self.cnt["AC"]

            def f(vector, i=i, acS=acS):
                self.w(vector, "AC", acS)
                self.inc(vector.tensor_reduce(
                    self.stat[:, i, 10:26].rearrange("p (h o) -> p h o", o=1),
                    self.sq[:, 0:D].rearrange("p (h v) -> p h v", h=NHG),
                    AX.X, ALU.add), "DV")
            self.VE(f)
            dvS = self.cnt["DV"]

            def f(scalar, i=i, dvS=dvS):
                self.w(scalar, "DV", dvS)
                self.inc(scalar.activation(
                    self.stat[:, i, 10:26], self.stat[:, i, 10:26],
                    AF.Sqrt, bias=self.epsc[:, 1:2]), "AC")
            self.SC(f)
            acS2 = self.cnt["AC"]

            def f(vector, i=i, acS2=acS2):
                self.w(vector, "AC", acS2)
                self.inc(vector.reciprocal(self.stat[:, i, 10:26],
                                           self.stat[:, i, 10:26]), "DV")
                self.w(vector, "DV")
                self.inc(vector.tensor_tensor(
                    self.ktm[:, i, :].rearrange("p (h v) -> p h v", h=NHG),
                    self.ktm[:, i, :].rearrange("p (h v) -> p h v", h=NHG),
                    self.stat[:, i, 10:26].rearrange(
                        "p (h o) -> p h o", o=1).broadcast_to([128, NHG, HD]),
                    ALU.mult), "DV")
            self.VE(f)
        dv_kn = self.cnt["DV"]

        self.emit_transpose_tile(self.ktm, self.kfm, ("DV", dv_kn))

    def emit_pack_and_ag(self, j):
        ac_all = self.cnt["AC"]
        dv_all = self.cnt["DV"]

        def f(scalar, j=j):
            self.w(scalar, "AC", ac_all)
            self.w(scalar, "DV", dv_all)
            cs = slice(j * TILE, (j + 1) * TILE)
            self.inc(scalar.dma_start(
                out=self.Msq[:, cs].rearrange("(g p) t -> p g t", p=128),
                in_=self.qfm[:, 0:4, :]), "qS", 16)
            self.inc(scalar.dma_start(
                out=self.Msk[:, cs].rearrange("(g p) t -> p g t", p=128),
                in_=self.kfm[:, 0:4, :]), "qS", 16)
            self.inc(scalar.dma_start(
                out=self.Msv[cs, :].rearrange("(i p) h -> p i h", p=128),
                in_=self.vtm[:, :, 0:512]), "qS", 16)
            self.inc(scalar.dma_start(
                out=self.Mslg[cs, :].rearrange("(i p) h -> p i h", p=128),
                in_=self.lgt[:, :, 0:16]), "qS", 16)
            ps_ = "qP%d" % j
            self.inc(scalar.dma_start(
                out=self.agq_i[j].rearrange("(g p) t -> p g t", p=128),
                in_=self.qfm[:, 4:8, :]), ps_, 16)
            self.inc(scalar.dma_start(
                out=self.agk_i[j].rearrange("(g p) t -> p g t", p=128),
                in_=self.kfm[:, 4:8, :]), ps_, 16)
            self.inc(scalar.dma_start(
                out=self.agv_i[j].rearrange("(i p) h -> p i h", p=128),
                in_=self.vtm[:, :, 512:1024]), ps_, 16)
            self.inc(scalar.dma_start(
                out=self.aglg_i[j].rearrange("(i p) h -> p i h", p=128),
                in_=self.lgt[:, :, 16:32]), ps_, 16)
        self.SC(f)

        def f(gpsimd, j=j):
            self.w(gpsimd, "qP%d" % j, 64)
            rg = [[0, 1], [2, 3], [4, 5], [6, 7]]
            for iin, iout in [(self.agq_i, self.agq_o),
                              (self.agk_i, self.agk_o),
                              (self.agv_i, self.agv_o),
                              (self.aglg_i, self.aglg_o)]:
                self.inc(gpsimd.collective_compute(
                    "AllGather", ALU.bypass, replica_groups=rg,
                    ins=[iin[j]], outs=[iout[j]]), "CC")
        self.GP(f)

    def emit_m_finalize(self):
        qs_at = self.cnt["qS"]

        def f(scalar, qs_at=qs_at):
            self.w(scalar, "qS", qs_at)
            self.inc(scalar.dma_start(
                out=self.Mq[DS(self.dmy, 1), :, :].squeeze(0),
                in_=self.Msq[...]), "qS", 16)
            self.inc(scalar.dma_start(
                out=self.Mk[DS(self.dmy, 1), :, :].squeeze(0),
                in_=self.Msk[...]), "qS", 16)
            self.inc(scalar.dma_start(
                out=self.Mv[DS(self.dmy, 1), :, :].squeeze(0),
                in_=self.Msv[...]), "qS", 16)
            self.inc(scalar.dma_start(
                out=self.Mlg[DS(self.dmy, 1), :, :].squeeze(0),
                in_=self.Mslg[...]), "qS", 16)
        self.SC(f)
        ccA = self.cnt["CC"]

        def f(gpsimd, ccA=ccA):
            self.w(gpsimd, "CC", ccA)
            self.inc(gpsimd.dma_start(
                out=self.Mq[DS(self.dot, 1), :, :].squeeze(0).rearrange(
                    "r (j t) -> j r t", t=TILE),
                in_=self.agq_o[:, DS(self.dot, 1), :, :].squeeze(1)),
                "qG", 16)
            self.inc(gpsimd.dma_start(
                out=self.Mk[DS(self.dot, 1), :, :].squeeze(0).rearrange(
                    "r (j t) -> j r t", t=TILE),
                in_=self.agk_o[:, DS(self.dot, 1), :, :].squeeze(1)),
                "qG", 16)
            self.inc(gpsimd.dma_start(
                out=self.Mv[DS(self.dot, 1), :, :].squeeze(0).rearrange(
                    "(j r) h -> j r h", r=TILE),
                in_=self.agv_o[:, DS(self.dot, 1), :, :].squeeze(1)),
                "qG", 16)
            self.inc(gpsimd.dma_start(
                out=self.Mlg[DS(self.dot, 1), :, :].squeeze(0).rearrange(
                    "(j r) h -> j r h", r=TILE),
                in_=self.aglg_o[:, DS(self.dot, 1), :, :].squeeze(1)),
                "qG", 16)
        self.GP(f)
        self.m_ready_qs = self.cnt["qS"]
        self.m_ready_qg = self.cnt["qG"]

    def emit_mixer_chunk(self, c):
        upto = getattr(self, "mixer_upto", 99)
        need_ag = (c // 4) % NTILE
        pe_pc = self.cnt["PE"]
        dv_pc = self.cnt["DV"]

        def f(scalar, c=c, need_ag=need_ag):
            self.w(scalar, "qS", self.m_ready_qs)
            self.w(scalar, "qG", self.m_ready_qg)
            if c > 0:
                self.w(scalar, "PE", pe_pc)
                self.w(scalar, "DV", dv_pc)
            hf = c // 32
            r0 = (c % 32) * CH
            self.inc(scalar.dma_start(
                out=self.kq[:, 0, :, :],
                in_=self.Mk[hf].rearrange("(h p) t -> p h t", p=64)[
                    :, :, r0:r0 + CH]), "qS", 16)
            self.inc(scalar.dma_start(
                out=self.kq[:, 1, :, :],
                in_=self.Mq[hf].rearrange("(h p) t -> p h t", p=64)[
                    :, :, r0:r0 + CH]), "qS", 16)
            self.inc(scalar.dma_start(
                out=self.vch[...], in_=self.Mv[hf, r0:r0 + CH, :]),
                "qS", 16)
            self.inc(scalar.dma_start(
                out=self.lgch[...], in_=self.Mlg[hf, r0:r0 + CH, :]),
                "qS", 16)
        self.SC(f)
        qM = self.cnt["qS"]

        def f(tensor, qM=qM):
            self.w(tensor, "qS", qM)
            self.w(tensor, "DV")
            self.w(tensor, "AC")
            self.inc(tensor.matmul(self.pS[:, 0:8], self.tri[...],
                                   self.lgch[:, 0:8], start=True, stop=True),
                     "PE")
        self.TE(f)
        pe1 = self.cnt["PE"]

        def f(vector, pe1=pe1):
            self.w(vector, "PE", pe1)
            for wsem in ["OD0", "OD1", "OD2", "OD3"]:
                self.w(vector, wsem)
            vector.tensor_copy(self.lalbtm[:, 0:8], self.pS[:, 0:8])
            self.inc(vector.tensor_tensor(self.lalbtm[:, 8:16],
                                          self.pS[:, 0:8],
                                          self.lgch[:, 8:16], ALU.subtract),
                     "DV")
        self.VE(f)
        dv1 = self.cnt["DV"]
        ac1 = self.cnt["AC"]

        def f(scalar, dv1=dv1):
            self.w(scalar, "DV", dv1)
            self.inc(scalar.activation(self.gam[...], self.lalbtm[:, 0:8],
                                       AF.Exp), "AC")
            self.inc(scalar.dma_start(out=self.laend[...],
                                      in_=self.lalbtm[127:128, 0:8]),
                     "qS", 16)
        self.SC(f)
        ac2 = self.cnt["AC"]
        qE = self.cnt["qS"]

        def f(vector, ac2=ac2):
            self.w(vector, "AC", ac2)
            self.inc(vector.tensor_scalar(self.ngam[...], self.gam[...],
                                          -1.0, None, ALU.mult), "DV")
        self.VE(f)

        def f(tensor, dv1=dv1, ac1=ac1, qE=qE):
            self.w(tensor, "DV", dv1)
            self.w(tensor, "AC", ac1)
            self.inc(tensor.transpose(self.pS[0:16, 128:256],
                                      self.lalbtm[...], self.ident[...]),
                     "PE")
            self.w(tensor, "qS", qE)
            self.inc(tensor.matmul(self.pS[:, 256:264], self.ones1[...],
                                   self.laend[...], start=True, stop=True),
                     "PE")
        self.TE(f)
        pe2 = self.cnt["PE"]

        def f(scalar, pe2=pe2):
            self.w(scalar, "PE", pe2)
            self.inc(scalar.activation(self.lalbfm[...],
                                       self.pS[0:16, 128:256], AF.Copy), "AC")
            self.inc(scalar.activation(self.gend[...], self.pS[:, 256:264],
                                       AF.Exp), "AC")
        self.SC(f)
        ac3 = self.cnt["AC"]

        if upto < 1:
            return
        for hg in range(2):
            def f(tensor, hg=hg, ac3=ac3):
                if hg == 0:
                    self.w(tensor, "AC", ac3)
                else:
                    self.w(tensor, "AC")
                for h in range(hg * 4, hg * 4 + 4):
                    tensor.matmul(self.pM1[:, h % 4, :],
                                  self.indla[:, h * 128:(h + 1) * 128],
                                  self.lalbfm[...], start=True, stop=False)
                    ins = tensor.matmul(self.pM1[:, h % 4, :],
                                        self.lalbfm[...],
                                        self.negind[:, h * 128:(h + 1) * 128],
                                        start=False, stop=True)
                self.inc(ins, "PE")
            self.TE(f)
            peE = self.cnt["PE"]

            def f(vector, hg=hg, peE=peE):
                self.w(vector, "PE", peE)
                for h in range(hg * 4, hg * 4 + 4):
                    self.inc(vector.tensor_scalar(
                        self.pM1[:, h % 4, :], self.pM1[:, h % 4, :],
                        1.0, None, ALU.min), "DV")
            self.VE(f)
            dvE = self.cnt["DV"]

            def f(scalar, hg=hg, dvE=dvE):
                self.w(scalar, "DV", dvE)
                for h in range(hg * 4, hg * 4 + 4):
                    self.inc(scalar.activation(self.e2[:, h, :],
                                               self.pM1[:, h % 4, :], AF.Exp),
                             "AC")
            self.SC(f)
        acE2 = self.cnt["AC"]

        if upto < 2:
            return
        for h in range(NH):
            def f(tensor, h=h):
                if h == 0:
                    self.w(tensor, "AC", acE2)
                else:
                    self.w(tensor, "DV")
                self.inc(tensor.matmul(self.pM1[:, h % 2, :],
                                       self.kq[:, 0, h, :],
                                       self.kq[:, 0, h, :],
                                       start=True, stop=True), "PE")
                self.inc(tensor.matmul(self.pM1[:, 2 + h % 2, :],
                                       self.kq[:, 0, h, :],
                                       self.kq[:, 1, h, :],
                                       start=True, stop=True), "PE")
            self.TE(f)
            peKK = self.cnt["PE"]

            def f(vector, h=h, peKK=peKK):
                self.w(vector, "PE", peKK)
                vector.tensor_tensor(self.gt[:, h, :], self.e2[:, h, :],
                                     self.pM1[:, h % 2, :], ALU.mult)
                self.inc(vector.tensor_tensor(self.mt[:, h, :],
                                              self.e2[:, h, :],
                                              self.pM1[:, 2 + h % 2, :],
                                              ALU.mult), "DV")
            self.VE(f)
            dvKK = self.cnt["DV"]

            def f(gpsimd, h=h, dvKK=dvKK):
                self.w(gpsimd, "DV", dvKK)
                self.inc(gpsimd.tensor_tensor(self.gt[:, h, :],
                                              self.gt[:, h, :],
                                              self.negstrict[...], ALU.mult),
                         "GPc")
                self.inc(gpsimd.tensor_tensor(self.mt[:, h, :],
                                              self.mt[:, h, :],
                                              self.incl[...], ALU.mult),
                         "GPc")
            self.GP(f)

        if upto < 3:
            return
        for h in range(NH):
            gph = self.cnt["GPc"] - 2 * (NH - 1 - h)

            def f(tensor, h=h, gph=gph):
                self.w(tensor, "GPc", gph)
                if h >= 1:
                    self.w(tensor, "AC")
                self.inc(tensor.transpose(self.pM2[:, h % 4, :],
                                          self.gt[:, h, :], self.ident[...]),
                         "PE")
            self.TE(f)
            peH = self.cnt["PE"]

            def f(scalar, h=h, peH=peH):
                self.w(scalar, "PE", peH)
                self.inc(scalar.activation(self.ht[:, h, :],
                                           self.pM2[:, h % 4, :], AF.Copy),
                         "AC")
            self.SC(f)

        if upto < 4:
            return
        for h in range(NH):
            def f(tensor, h=h):
                if h >= 1:
                    self.w(tensor, "DV")
                self.inc(tensor.matmul(self.pM3[:, h % 4, :],
                                       self.kq[:, 0, h, :],
                                       self.svst[:, h, :],
                                       start=True, stop=True), "PE")
            self.TE(f)
            peR = self.cnt["PE"]

            def f(vector, h=h, peR=peR):
                self.w(vector, "PE", peR)
                self.inc(vector.scalar_tensor_tensor(
                    self.bt[:, h, :], self.pM3[:, h % 4, :],
                    self.ngam[:, h:h + 1],
                    self.vch[:, h * HD:(h + 1) * HD],
                    ALU.mult, ALU.add), "DV")
            self.VE(f)

        for h in range(NH):
            def f(tensor, h=h):
                self.w(tensor, "DV")
                self.inc(tensor.matmul(self.pM3[:, 4 + h % 4, :],
                                       self.gt[:, h, :], self.bt[:, h, :],
                                       start=True, stop=True), "PE")
            self.TE(f)
            peA0 = self.cnt["PE"]

            def f(vector, h=h, peA0=peA0):
                self.w(vector, "PE", peA0)
                self.inc(vector.tensor_tensor(self.bt[:, h, :],
                                              self.bt[:, h, :],
                                              self.pM3[:, 4 + h % 4, :],
                                              ALU.add), "DV")
            self.VE(f)

        if upto < 5:
            return
        cur, alt = self.gt, self.g2t
        for lev in range(1, 7):
            for h in range(NH):
                def f(tensor, h=h, cur=cur):
                    self.w(tensor, "AC")
                    self.w(tensor, "DV")
                    self.inc(tensor.matmul(self.pM1[:, h % 4, :],
                                           self.ht[:, h, :], cur[:, h, :],
                                           start=True, stop=True), "PE")
                self.TE(f)
                peG = self.cnt["PE"]

                def f(scalar, h=h, alt=alt, peG=peG):
                    self.w(scalar, "PE", peG)
                    self.inc(scalar.activation(alt[:, h, :],
                                               self.pM1[:, h % 4, :],
                                               AF.Copy), "AC")
                self.SC(f)
                acG = self.cnt["AC"]

                if lev < 6:
                    def f(tensor, h=h, alt=alt, acG=acG):
                        self.w(tensor, "AC", acG)
                        self.inc(tensor.transpose(self.pM2[:, h % 4, :],
                                                  alt[:, h, :],
                                                  self.ident[...]), "PE")
                    self.TE(f)
                    peH2 = self.cnt["PE"]

                    def f(scalar, h=h, peH2=peH2):
                        self.w(scalar, "PE", peH2)
                        self.inc(scalar.activation(self.ht[:, h, :],
                                                   self.pM2[:, h % 4, :],
                                                   AF.Copy), "AC")
                    self.SC(f)

                def f(tensor, h=h, alt=alt, acG=acG):
                    self.w(tensor, "AC", acG)
                    self.w(tensor, "DV")
                    self.inc(tensor.matmul(self.pM3[:, h % 4, :],
                                           alt[:, h, :], self.bt[:, h, :],
                                           start=True, stop=True), "PE")
                self.TE(f)
                peA = self.cnt["PE"]

                def f(vector, h=h, peA=peA):
                    self.w(vector, "PE", peA)
                    self.inc(vector.tensor_tensor(self.bt[:, h, :],
                                                  self.bt[:, h, :],
                                                  self.pM3[:, h % 4, :],
                                                  ALU.add), "DV")
                self.VE(f)
            cur, alt = alt, cur

        if upto < 6:
            return
        for h in range(NH):
            def f(tensor, h=h):
                self.w(tensor, "DV")
                if h >= 1:
                    self.w(tensor, "AC")
                self.inc(tensor.matmul(self.pM3[:, h % 2, :],
                                       self.mt[:, h, :], self.bt[:, h, :],
                                       start=True, stop=True), "PE")
                self.inc(tensor.matmul(self.pM3[:, 2 + h % 2, :],
                                       self.kq[:, 1, h, :],
                                       self.svst[:, h, :],
                                       start=True, stop=True), "PE")
            self.TE(f)
            peO = self.cnt["PE"]

            def f(scalar, h=h, peO=peO):
                self.w(scalar, "PE", peO)
                self.w(scalar, "DV")
                self.inc(scalar.activation(self.tq[:, h, :],
                                           self.pM3[:, 2 + h % 2, :],
                                           AF.Copy), "AC")
            self.SC(f)
            acO = self.cnt["AC"]

            def f(vector, h=h, acO=acO):
                self.w(vector, "AC", acO)
                self.inc(vector.scalar_tensor_tensor(
                    self.osb[:, h * HD:(h + 1) * HD], self.tq[:, h, :],
                    self.gam[:, h:h + 1], self.pM3[:, h % 2, :],
                    ALU.mult, ALU.add), "DV")
            self.VE(f)

        if upto < 7:
            return
        for h in range(NH):
            def f(vector, h=h):
                self.inc(vector.tensor_scalar(
                    self.usc[:, h, :], self.bt[:, h, :],
                    self.e2[:, h, 127:128], None, ALU.mult), "DV")
            self.VE(f)
            dvUs = self.cnt["DV"]

            def f(tensor, h=h, dvUs=dvUs):
                self.w(tensor, "DV", dvUs)
                if h >= 1:
                    self.w(tensor, "AC")
                self.inc(tensor.transpose(self.pM2[:, h % 4, 0:64],
                                          self.kq[:, 0, h, :],
                                          self.ident[0:64, 0:64]), "PE")
            self.TE(f)
            peT = self.cnt["PE"]

            def f(scalar, h=h, peT=peT):
                self.w(scalar, "PE", peT)
                self.inc(scalar.activation(self.ktmh[:, h, :],
                                           self.pM2[:, h % 4, 0:64],
                                           AF.Copy), "AC")
            self.SC(f)
            acT = self.cnt["AC"]

            def f(tensor, h=h, acT=acT):
                self.w(tensor, "AC", acT)
                self.w(tensor, "DV")
                self.inc(tensor.matmul(self.pM3[0:64, 4 + h % 2, :],
                                       self.ktmh[:, h, :], self.usc[:, h, :],
                                       start=True, stop=True), "PE")
            self.TE(f)
            peS = self.cnt["PE"]

            def f(vector, h=h, peS=peS):
                self.w(vector, "PE", peS)
                self.inc(vector.scalar_tensor_tensor(
                    self.svst[:, h, :], self.svst[:, h, :],
                    self.gend[0:64, h:h + 1],
                    self.pM3[0:64, 4 + h % 2, :],
                    ALU.mult, ALU.add), "DV")
            self.VE(f)

        dvSt = self.cnt["DV"]

        def f(scalar, c=c, dvSt=dvSt):
            self.w(scalar, "DV", dvSt)
            self.inc(scalar.dma_start(
                out=self.o_all[c // 32, (c % 32) * CH:(c % 32) * CH + CH, :],
                in_=self.osb[...]), "OD%d" % ((c % 32) // 8), 16)
        self.SC(f)

    def emit_o_exchange(self, jj):
        W = NT // 4

        def f(scalar, jj=jj):
            self.w(scalar, "OD%d" % jj, 16 * 16)
            self.inc(scalar.dma_start(
                out=self.ago_i[jj],
                in_=self.o_all[DS(self.dot, 1),
                               jj * W:(jj + 1) * W, :].squeeze(0)),
                "qA%d" % jj, 16)
        self.SC(f)

        def f(gpsimd, jj=jj):
            self.w(gpsimd, "qA%d" % jj, 16)
            self.inc(gpsimd.collective_compute(
                "AllGather", ALU.bypass,
                replica_groups=[[0, 1], [2, 3], [4, 5], [6, 7]],
                ins=[self.ago_i[jj]], outs=[self.ago_o[jj]]), "CO")
        self.GP(f)
        co = self.cnt["CO"]

        self.co_last = self.cnt["CO"]

    def emit_mo_assembly(self):
        def f(sync):
            for wsem in ["OD0", "OD1", "OD2", "OD3"]:
                self.w(sync, wsem, 16 * 16)
            self.w(sync, "CO", self.co_last)
            self.inc(sync.dma_start(
                out=self.Mo[:, 0:512],
                in_=self.o_all[DS(self.dmy, 1), :, :].squeeze(0)), "qMO", 16)
            self.inc(sync.dma_start(
                out=self.Mo[:, 512:1024].rearrange("(j r) h -> j r h",
                                                   r=NT // 4),
                in_=self.ago_o[:, DS(self.dot, 1), :, :].squeeze(1)),
                "qMO", 16)
        self.SY(f)
        self.mo_done_all = self.cnt["qMO"]

    def emit_tile_C(self, jj):
        dv_prev = self.cnt["DV"]
        ac_prev = self.cnt["AC"]

        def f(sync, jj=jj):
            self.w(sync, "qMO", self.mo_done_all)
            self.w(sync, "qX1")
            if jj > 0:
                self.w(sync, "DV", dv_prev)
                self.w(sync, "AC", ac_prev)
            self.inc(sync.dma_start(
                out=self.ktm[...],
                in_=self.Mo[jj * TILE:(jj + 1) * TILE, :].rearrange(
                    "(i p) d -> p i d", p=128)), "qX", 16)
            self.inc(sync.dma_start(
                out=self.xres[...],
                in_=self.x1h[jj * TILE:(jj + 1) * TILE, :].rearrange(
                    "(i p) d -> p i d", p=128)), "qX", 16)
        self.SY(f)
        qm = self.cnt["qX"]

        for i in range(TC):
            def f(scalar, i=i, qm=qm):
                if i == 0:
                    self.w(scalar, "qX", qm)
                else:
                    self.w(scalar, "DV")
                self.inc(scalar.activation(self.sq[:, 0:D], self.ktm[:, i, :],
                                           AF.Square), "AC")
            self.SC(f)
            acS = self.cnt["AC"]

            def f(vector, i=i, acS=acS, qm=qm):
                if i == 0:
                    self.w(vector, "qX", qm)
                self.w(vector, "AC", acS)
                self.inc(vector.tensor_reduce(
                    self.stat[:, i, 10:26].rearrange("p (h o) -> p h o", o=1),
                    self.sq[:, 0:D].rearrange("p (h v) -> p h v", h=NHG),
                    AX.X, ALU.add), "DV")
            self.VE(f)
            dvS = self.cnt["DV"]

            def f(scalar, i=i, dvS=dvS):
                self.w(scalar, "DV", dvS)
                self.inc(scalar.activation(
                    self.stat[:, i, 10:26], self.stat[:, i, 10:26],
                    AF.Sqrt, bias=self.epsc[:, 2:3], scale=1.0 / HD), "AC")
            self.SC(f)
            acS2 = self.cnt["AC"]

            def f(vector, i=i, acS2=acS2):
                self.w(vector, "AC", acS2)
                self.inc(vector.reciprocal(self.stat[:, i, 10:26],
                                           self.stat[:, i, 10:26]), "DV")
                self.w(vector, "DV")
                vector.tensor_tensor(
                    self.ktm[:, i, :].rearrange("p (h v) -> p h v", h=NHG),
                    self.ktm[:, i, :].rearrange("p (h v) -> p h v", h=NHG),
                    self.stat[:, i, 10:26].rearrange(
                        "p (h o) -> p h o", o=1).broadcast_to([128, NHG, HD]),
                    ALU.mult)
                self.inc(vector.tensor_tensor(
                    self.ktm[:, i, :], self.ktm[:, i, :], self.nwrep[...],
                    ALU.mult), "DV")
            self.VE(f)
        dv_on = self.cnt["DV"]

        self.emit_transpose_tile(self.ktm, self.qfm, ("DV", dv_on))
        ac_on = self.cnt["AC"]

        for half in range(2):
            buf = half

            wsem = "qW%d" % buf

            def f(sync, half=half, buf=buf, wsem=wsem):
                self.w(sync, "PE")
                self.inc(sync.dma_start(
                    out=self.wbuf[:, buf, :, :],
                    in_=self.wo.rearrange("(c p) o -> p c o", p=128)[
                        :, :, half * 512:half * 512 + 512]), wsem, 16)
            self.SY(f)
            qw = self.cnt[wsem]

            for i in range(TC):
                ps = [self.pA, self.pB][i % 2]

                def f(tensor, i=i, half=half, buf=buf, qw=qw, ps=ps,
                      wsem=wsem):
                    if i == 0:
                        self.w(tensor, wsem, qw)
                        self.w(tensor, "AC", ac_on)
                    if i >= 2:
                        self.w(tensor, "DV")
                    for d in range(DC):
                        ins = tensor.matmul(
                            ps[:, 0:512],
                            self.qfm[:, d, i * 128:(i + 1) * 128],
                            self.wbuf[:, buf, d, :],
                            start=(d == 0), stop=(d == DC - 1))
                    self.inc(ins, "PE")
                self.TE(f)
                peW = self.cnt["PE"]

                def f(vector, i=i, half=half, ps=ps, peW=peW):
                    self.w(vector, "PE", peW)
                    self.inc(vector.tensor_tensor(
                        self.xres[:, i, half * 512:half * 512 + 512],
                        self.xres[:, i, half * 512:half * 512 + 512],
                        ps[:, 0:512], ALU.add), "DV")
                self.VE(f)

        self.emit_ln(dvgate=True)
        self.emit_router()
        self.emit_ffn(lambda e, half: self.ew[e].rearrange(
            "(c p) o -> p c o", p=128)[:, :, half * 512:half * 512 + 512])
        dv_x3 = self.cnt["DV"]

        def f(sync, jj=jj, dv_x3=dv_x3):
            self.w(sync, "DV", dv_x3)
            self.inc(sync.dma_start(
                out=self.yout[jj * TILE:(jj + 1) * TILE, :].rearrange(
                    "(i p) d -> p i d", p=128),
                in_=self.xres[...]), "ST", 16)
        self.SY(f)

    def emit_final(self):
        if getattr(self, "mixdbg", False):
            names = ["e2", "gt", "g2t", "ht", "mt", "bt", "usc", "ktmh",
                     "svst", "lalbtm", "lalbfm", "gam", "ngam", "gend",
                     "kq", "vch", "lgch", "osb", "stat", "w05", "mcol",
                     "mraw", "acc", "lnfm", "lntm", "xres"]

            def f(sync, names=names):
                for s in ["qS", "qG", "PE", "DV", "AC", "GPc"]:
                    self.w(sync, s)
                for wsem in ["OD0", "OD1", "OD2", "OD3"]:
                    self.w(sync, wsem)
                for nm in names:
                    t = getattr(self, nm)
                    shp = list(t[...].shape)
                    d = self.nc.dram_tensor("dbx_" + nm, shp, F32,
                                            kind="ExternalOutput").ap()
                    self.inc(sync.dma_start(out=d[...], in_=t[...]),
                             "ST", 16)
            self.SY(f)
        if self.debug:
            def f(sync):
                self.w(sync, "qS")
                self.w(sync, "qG")
                self.w(sync, "qMO")
                for wsem in ["OD0", "OD1", "OD2", "OD3"]:
                    self.w(sync, wsem)
                for j in range(NTILE):
                    self.w(sync, "qP%d" % j)
                for j in range(4):
                    self.w(sync, "qA%d" % j)
                for dn, src in [("dbg_x1", self.x1h), ("dbg_mq", self.Mq),
                                ("dbg_mk", self.Mk), ("dbg_mv", self.Mv),
                                ("dbg_mlg", self.Mlg), ("dbg_o", self.o_all),
                                ("dbg_mo", self.Mo)]:
                    self.inc(sync.dma_start(out=self.dbg[dn][...],
                                            in_=src[...]), "ST", 16)
            self.SY(f)

        def f(sync):
            self.w(sync, "ST")
        self.SY(f)


def _host_inputs(inputs, core):
    par = core % 2
    b = core // 2
    x = np.ascontiguousarray(
        inputs["L0_attn_x"][b, par * NT:(par + 1) * NT, :], dtype=np.float32)
    if par == 0:
        dperm = np.arange(D)
        hperm = np.arange(NHG)
    else:
        dperm = np.concatenate([np.arange(512, 1024), np.arange(0, 512)])
        hperm = np.concatenate([np.arange(8, 16), np.arange(0, 8)])
    wq = np.ascontiguousarray(np.asarray(inputs["L0_attn_Wq"])[:, dperm],
                              np.float32)
    wk = np.ascontiguousarray(np.asarray(inputs["L0_attn_Wk"])[:, dperm],
                              np.float32)
    wv = np.ascontiguousarray(np.asarray(inputs["L0_attn_Wv"])[:, dperm],
                              np.float32)
    wo = np.ascontiguousarray(np.asarray(inputs["L0_attn_Wo"])[dperm, :],
                              np.float32)
    wbet = np.ascontiguousarray(np.asarray(inputs["L0_attn_Wbet"])[:, hperm],
                                np.float32)
    wf = np.ascontiguousarray(np.asarray(inputs["L0_attn_Wf"])[:, hperm],
                              np.float32)
    nw = np.ascontiguousarray(np.asarray(inputs["L0_attn_norm_w"])[dperm],
                              np.float32).reshape(1, D)
    ew = np.ascontiguousarray(
        np.asarray(inputs["L0_ffn_expert_w"]).transpose(0, 2, 1), np.float32)
    ebd = np.ascontiguousarray(np.asarray(inputs["L0_ffn_expert_b"]),
                               np.float32)
    rwt = np.ascontiguousarray(np.asarray(inputs["L0_ffn_router_w"]).T,
                               np.float32)
    rb = np.ascontiguousarray(np.asarray(inputs["L0_ffn_router_b"]),
                              np.float32).reshape(1, E)
    s = np.arange(128)[:, None]
    t = np.arange(128)[None, :]
    c_indla = np.zeros((NHG, NH * 128), np.float32)
    c_negind = np.zeros((NHG, NH * 128), np.float32)
    for h in range(NH):
        c_indla[h, h * 128:(h + 1) * 128] = 1.0
        c_negind[8 + h, h * 128:(h + 1) * 128] = -1.0
    return {
        "xin": x, "ew": ew, "ebd": ebd, "rwt": rwt, "rb": rb,
        "wq": wq, "wk": wk, "wv": wv, "wo": wo, "wbet": wbet, "wf": wf,
        "nw": nw,
        "c_ident": np.eye(128, dtype=np.float32),
        "c_tri": (s <= t).astype(np.float32),
        "c_incl": (s <= t).astype(np.float32),
        "c_negstrict": -(s < t).astype(np.float32),
        "c_ones1": np.ones((1, 128), np.float32),
        "c_indla": c_indla, "c_negind": c_negind,
    }


_NC_CACHE = {}


def build_program(debug=False, stages="AMOC"):
    key = (bool(debug), stages)
    if key not in _NC_CACHE:
        _NC_CACHE[key] = KB(debug=debug, stages=stages).build()
    return _NC_CACHE[key]


def kernel(**inputs):
    nc = build_program(debug=False)
    in_maps = [_host_inputs(inputs, c) for c in range(8)]
    res = run_bass_kernel_spmd(nc, in_maps, core_ids=list(range(8)))
    out = np.empty((4, SEQ, D), np.float32)
    for c in range(8):
        par = c % 2
        b = c // 2
        out[b, par * NT:(par + 1) * NT, :] = res.results[c]["yout"]
    return out
